# version 01
# baseline (speedup 1.0000x reference)
import numpy as np
import ml_dtypes
import concourse.bass as bass
import concourse.mybir as mybir
from concourse.bass_utils import run_bass_kernel_spmd

F32 = mybir.dt.float32
BF16 = mybir.dt.bfloat16
FP8 = mybir.dt.float8e4
AF = mybir.ActivationFunctionType
ALU = mybir.AluOpType

NCORES = 8
T = 4096
NT = 32
NSC = 4
E = 1024
H = 16

ENGS = ("tensor", "scalar", "vector", "gpsimd", "sync")


class Prog:

    def __init__(self, nc):
        self.nc = nc
        self.sems = {}
        self.cnt = {}
        self.items = {e: [] for e in ENGS}
        self.last_w = {}
        self.readers = {}
        self.waited = {e: {} for e in ENGS}
        self.dma_sems = set()

    def sem(self, name):
        if name not in self.sems:
            self.sems[name] = self.nc.alloc_semaphore(name=name)
            self.cnt[name] = 0
        return self.sems[name]

    def _need(self, eng, sem_name, val, waits):
        if val <= 0:
            return
        w = self.waited[eng]
        if w.get(sem_name, 0) >= val:
            return
        w[sem_name] = val
        waits.append((sem_name, val))

    def op(self, eng, fn, reads=(), writes=(), sem_name=None, inc=0):
        if sem_name is None:
            sem_name = "e_" + eng
        self.sem(sem_name)
        waits = []

        def dep(sname, v):
            if sname in self.dma_sems:
                v = self.cnt[sname]
            self._need(eng, sname, v, waits)

        for k in reads:
            lw = self.last_w.get(k)
            if lw:
                dep(lw[0], lw[1])
        for k in writes:
            lw = self.last_w.get(k)
            if lw:
                dep(lw[0], lw[1])
            for sname, v in self.readers.get(k, {}).items():
                dep(sname, v)
        val = None
        if inc:
            self.cnt[sem_name] += inc
            val = self.cnt[sem_name]
        sems = self.sems

        def thunk(h, waits=waits, fn=fn, sem_name=sem_name, inc=inc):
            for sn, v in waits:
                h.wait_ge(sems[sn], v)
            ins = fn(h)
            if inc and ins is not None:
                ins.then_inc(sems[sem_name], inc)

        self.items[eng].append(thunk)
        if inc:
            for k in reads:
                self.readers.setdefault(k, {})[sem_name] = val
            for k in writes:
                self.last_w[k] = (sem_name, val)
                self.readers[k] = {}
        return val

    def raw(self, eng, fn):
        self.items[eng].append(fn)

    def flush(self):
        nc = self.nc
        items = self.items
        with nc.Block() as blk:
            if items["gpsimd"]:
                blk.gpsimd(lambda h, l=tuple(items["gpsimd"]): [f(h) for f in l])
            if items["sync"]:
                blk.sync(lambda h, l=tuple(items["sync"]): [f(h) for f in l])
            if items["tensor"]:
                blk.tensor(lambda h, l=tuple(items["tensor"]): [f(h) for f in l])
            if items["scalar"]:
                blk.scalar(lambda h, l=tuple(items["scalar"]): [f(h) for f in l])
            if items["vector"]:
                blk.vector(lambda h, l=tuple(items["vector"]): [f(h) for f in l])
        self.items = {e: [] for e in ENGS}


def build_nc():
    nc = bass.Bass("TRN2", target_bir_lowering=False)

    x = nc.dram_tensor("x", [T, E], F32, kind="ExternalInput").ap()
    wat = nc.dram_tensor("wat", [3 * E, E], F32, kind="ExternalInput").ap()
    wpr = nc.dram_tensor("wpr", [E, E], F32, kind="ExternalInput").ap()
    fk = nc.dram_tensor("fk", [E, E], BF16, kind="ExternalInput").ap()
    s_in = nc.dram_tensor("s_in", [128, 1], F32, kind="ExternalInput").ap()
    s_k = nc.dram_tensor("s_k", [128, 1], F32, kind="ExternalInput").ap()
    fbr = nc.dram_tensor("fbr", [128, E], BF16, kind="ExternalInput").ap()
    mkv = nc.dram_tensor("mkv", [128, 1], F32, kind="ExternalInput").ap()
    out = nc.dram_tensor("out", [T, E], F32, kind="ExternalOutput").ap()

    d_xnb = nc.dram_tensor("d_xnb", [128, 8, T], BF16).ap()
    d_k = nc.dram_tensor("d_k", [128, NT, 1024], BF16).ap()
    d_v = nc.dram_tensor("d_v", [128, NT, 1040], BF16).ap()
    d_pf = nc.dram_tensor("d_pf", [128, NT, 520], F32).ap()
    cc_in = nc.dram_tensor("cc_in", [1024, 65], F32).ap()
    cc_out = nc.dram_tensor("cc_out", [2048, 65], F32).ap()

    from contextlib import ExitStack
    ctx = ExitStack()
    sbuf = lambda n, s, d: ctx.enter_context(nc.sbuf_tensor(n, s, d))
    psum = lambda n, s, d: ctx.enter_context(nc.psum_tensor(n, s, d))
    with ctx:
        wq_ar = sbuf("wq_ar", [128, 16384], BF16)
        xnb = sbuf("xnb", [128, 8192], BF16)
        watq = sbuf("watq", [128, 8192], BF16)
        wprT = sbuf("wprT", [128, 8192], BF16)
        kq = sbuf("kq", [128, 8192], FP8)
        xq = sbuf("xq", [128, 8192], FP8)
        k_sb = sbuf("k_sb", [128, 8192], BF16)
        v_sb = sbuf("v_sb", [128, 8320], BF16)
        pf_sb = sbuf("pf_sb", [128, 4160], F32)
        pf_bf = sbuf("pf_bf", [128, 4160], BF16)
        carry = sbuf("carry", [128, 520], F32)
        sinit = sbuf("sinit", [128, 520], F32)
        sinit_bf = sbuf("sinit_bf", [128, 520], BF16)
        xt = sbuf("xt", [128, 2048], F32)
        outt = sbuf("outt", [128, 2048], F32)
        outt2 = sbuf("outt2", [128, 2048], F32)
        scm = sbuf("scm", [128, 2048], BF16)
        kf = sbuf("kf", [128, 1024], BF16)
        yt = sbuf("yt", [128, 1024], BF16)
        wstg = sbuf("wstg", [128, 2048], BF16)
        psc = sbuf("psc", [128, 2048], F32)
        st = sbuf("st", [128, 64], F32)
        cols = sbuf("cols", [128, 8], F32)
        iot = sbuf("iot", [128, 128], F32)
        ident = sbuf("ident", [128, 128], BF16)
        mask = sbuf("mask", [128, 128], BF16)
        fbb = sbuf("fbb", [128, 1024], BF16)

        pA = psum("pA", [128, 512], F32)
        pB = psum("pB", [128, 512], F32)
        pC = psum("pC", [128, 512], F32)
        pD = psum("pD", [128, 512], F32)
        pS = psum("pS", [128, 1024], F32)
        tr1 = psum("tr1", [128, 1024], BF16)
        tr2 = psum("tr2", [128, 1024], BF16)

        P = Prog(nc)
        LD, STQ, CC = "q_ld", "q_st", "q_cc"

        P.dma_sems.update([LD, STQ])

        def dma(queue, dst, src, reads, writes):
            P.op(queue, lambda h, d=dst, s=src: h.dma_start(d, s),
                 reads=reads, writes=writes,
                 sem_name=(LD if queue == "sync" else STQ), inc=16)

        P.op("gpsimd", lambda h: h.iota(iot[:, :], [[1, 128]], channel_multiplier=-1,
                                        allow_small_or_imprecise_dtypes=True),
             writes=["iot"], inc=1)
        P.op("vector", lambda h: h.tensor_single_scalar(mask[:, :], iot[:, :], 0.0, ALU.is_ge),
             reads=["iot"], writes=["mask"], inc=1)
        P.op("vector", lambda h: h.tensor_single_scalar(ident[:, :], iot[:, :], 0.0, ALU.is_equal),
             reads=["iot"], writes=["ident"], inc=1)
        P.op("vector", lambda h: h.memset(cols[:, 0:1], -0.5), writes=["c_nh"], inc=1)
        dma("sync", st[:, 0:1], s_in, [], ["st_sc"])
        dma("sync", st[:, 1:2], s_k, [], ["st_sc"])
        dma("sync", st[:, 2:3], mkv, [], ["st_mk"])
        dma("sync", fbb[:, :], fbr, [], ["fbb"])
        P.op("vector", lambda h: h.reciprocal(cols[:, 1:2], st[:, 0:1]),
             reads=["st_sc"], writes=["c_si"], inc=1)
        P.op("vector", lambda h: h.reciprocal(cols[:, 2:3], st[:, 1:2]),
             reads=["st_sc"], writes=["c_sk"], inc=1)
        P.op("vector", lambda h: h.tensor_mul(cols[:, 3:4], st[:, 0:1], st[:, 1:2]),
             reads=["st_sc"], writes=["c_sp"], inc=1)

        def w_slice(kind, r, trp, trkey):
            if kind == "kv":
                src = wat[1024 + 128 * r:1024 + 128 * (r + 1), :]
            elif kind == "q":
                src = wat[128 * r:128 * (r + 1), :]
            else:
                src = wpr[128 * r:128 * (r + 1), :]
            slot = r % 2
            stg = wstg[:, 1024 * slot:1024 * (slot + 1)]
            skey = f"wstg{slot}"
            dma("gpsimd", stg, src, [], [skey])
            for e in range(8):
                P.op("tensor",
                     lambda h, e=e, stg=stg, trp=trp: h.transpose(
                         trp[:, 128 * e:128 * (e + 1)], stg[:, 128 * e:128 * (e + 1)],
                         ident[:, :]),
                     reads=[skey, "ident"],
                     writes=[trkey] if e in (0, 7) else [],
                     inc=1 if e in (0, 7) else 0)
            dstt, stride, dkey = {"kv": (wq_ar, 2048, "w_kv"),
                                  "q": (watq, 1024, "w_q"),
                                  "pr": (wprT, 1024, "w_pr")}[kind]
            base = 128 * r
            P.op("vector",
                 lambda h, trp=trp, dstt=dstt, stride=stride, base=base: h.tensor_copy(
                     dstt[:, :].rearrange("p (ec c) -> p ec c", c=stride)[:, :, base:base + 128],
                     trp[:, :].rearrange("p (ec c) -> p ec c", ec=8)),
                 reads=[trkey], writes=[dkey], inc=1)

        for r in range(16):
            w_slice("kv", r, tr1, "tr1")
        for r in range(8):
            w_slice("q", r, tr2, "tr2")
        for r in range(8):
            w_slice("pr", r, tr2, "tr2")
        for e in range(8):
            slot = e % 2
            stg = wstg[:, 1024 * slot:1024 * (slot + 1)]
            skey = f"wstg{slot}"
            dma("gpsimd", stg, fk[128 * e:128 * (e + 1), :], [], [skey])
            P.op("scalar",
                 lambda h, e=e, stg=stg: h.activation(
                     kq[:, 1024 * e:1024 * (e + 1)], stg, AF.Copy, scale=cols[:, 2:3]),
                 reads=[skey, "c_sk"], writes=["kq"], inc=1)

        kvbank = [(pA, "pA"), (pB, "pB"), (pC, "pC"), (pD, "pD")]

        def s_region(hh):
            g, par = hh // 2, hh % 2
            off = 65 * g if g < 7 else 512 + 65 * (g - 7)
            key = "pS0" if g < 7 else "pS1"
            return pS[64 * par:64 * (par + 1), off:off + 65], key, (0, 64 * par)

        def pf_col(g):
            return 65 * g

        for t in range(NT):
            sc, tl = divmod(t, 8)
            slot = t % 2
            xts = xt[:, 1024 * slot:1024 * (slot + 1)]
            xkey = f"xt{slot}"
            dma("sync", xts, x[128 * t:128 * (t + 1), :], [], [xkey])
            P.op("vector", lambda h, xts=xts: h.bn_stats(st[:, 16:22], xts[:, 0:512]),
                 reads=[xkey], writes=["stA"], inc=1)
            P.op("vector", lambda h, xts=xts: h.bn_stats(st[:, 22:28], xts[:, 512:1024]),
                 reads=[xkey], writes=["stA"], inc=1)
            P.op("vector", lambda h: h.bn_aggr(st[:, 12:14], st[:, 16:28]),
                 reads=["stA"], writes=["stB"], inc=1)
            P.op("vector", lambda h: h.tensor_scalar_add(st[:, 14:15], st[:, 13:14], 1e-5),
                 reads=["stB"], writes=["stC"], inc=1)
            P.op("scalar", lambda h: h.activation(st[:, 15:16], st[:, 14:15], AF.Ln),
                 reads=["stC"], writes=["stD"], inc=1)
            P.op("scalar", lambda h: h.activation(st[:, 16:17], st[:, 15:16], AF.Exp,
                                                  scale=cols[:, 0:1]),
                 reads=["stD", "c_nh"], writes=["stE"], inc=1)
            P.op("vector", lambda h: h.tensor_mul(st[:, 17:18], st[:, 16:17], st[:, 16:17]),
                 reads=["stE"], writes=["stF"], inc=1)
            P.op("vector", lambda h: h.tensor_mul(st[:, 18:19], st[:, 17:18], st[:, 14:15]),
                 reads=["stF", "stC"], writes=["stG"], inc=1)
            P.op("vector", lambda h: h.tensor_scalar(st[:, 19:20], st[:, 18:19], -0.5, 1.5,
                                                     ALU.mult, ALU.add),
                 reads=["stG"], writes=["stH"], inc=1)
            P.op("vector", lambda h: h.tensor_mul(st[:, 20:21], st[:, 19:20], st[:, 16:17]),
                 reads=["stH", "stE"], writes=["stI"], inc=1)
            P.op("vector", lambda h, xts=xts: h.tensor_scalar(
                yt[:, :], xts, st[:, 12:13], st[:, 20:21], ALU.subtract, ALU.mult),
                 reads=[xkey, "stB", "stI"], writes=["yt"], inc=1)
            for e in range(8):
                P.op("tensor", lambda h, e=e: h.transpose(
                    tr1[:, 128 * e:128 * (e + 1)], yt[:, 128 * e:128 * (e + 1)], ident[:, :]),
                     reads=["yt", "ident"], writes=["tr1"] if e in (0, 7) else [],
                     inc=1 if e in (0, 7) else 0)
            P.op("vector", lambda h, tl=tl: h.tensor_copy(
                xnb[:, :].rearrange("p (ec c) -> p ec c", ec=8)[:, :, 128 * tl:128 * (tl + 1)],
                tr1[:, :].rearrange("p (ec c) -> p ec c", ec=8)),
                 reads=["tr1"], writes=[f"xnb{tl}"], inc=1)
            for n in range(4):
                bank, bkey = kvbank[n]
                for e in range(8):
                    P.op("tensor", lambda h, e=e, n=n, tl=tl, bank=bank: h.matmul(
                        bank[:, :],
                        xnb[:, 1024 * e + 128 * tl:1024 * e + 128 * (tl + 1)],
                        wq_ar[:, 2048 * e + 512 * n:2048 * e + 512 * (n + 1)],
                        start=(e == 0), stop=(e == 7), skip_group_check=True),
                         reads=[f"xnb{tl}", "w_kv"],
                         writes=[bkey] if e in (0, 7) else [],
                         inc=1 if e in (0, 7) else 0)
                if n < 2:
                    pe = psc[:, 1024 * (n % 2):1024 * (n % 2) + 512]
                    pr = psc[:, 1024 * (n % 2) + 512:1024 * (n % 2) + 1024]
                    pky = f"psc{n % 2}"
                    P.op("scalar", lambda h, bank=bank, pe=pe: h.activation(pe, bank[:, :], AF.Exp),
                         reads=[bkey], writes=[pky + "e"], inc=1)
                    P.op("scalar", lambda h, bank=bank, pr=pr: h.activation(pr, bank[:, :], AF.Relu),
                         reads=[bkey], writes=[pky + "r"], inc=1)
                    P.op("vector", lambda h, pe=pe, pr=pr, tl=tl, n=n: h.scalar_tensor_tensor(
                        k_sb[:, 1024 * tl + 512 * n:1024 * tl + 512 * (n + 1)],
                        pe, 1.0, pr, ALU.min, ALU.add),
                         reads=[pky + "e", pky + "r", bkey], writes=[f"k{tl}"], inc=1)
                else:
                    hh0 = 8 * (n - 2)
                    P.op("vector", lambda h, bank=bank, tl=tl, hh0=hh0: h.tensor_copy(
                        v_sb[:, 1040 * tl:1040 * (tl + 1)].rearrange(
                            "p (hh ee) -> p hh ee", hh=16)[:, hh0:hh0 + 8, 0:64],
                        bank[:, :].rearrange("p (hh ee) -> p hh ee", hh=8)),
                         reads=[bkey], writes=[f"v{tl}"], inc=1)
            P.op("vector", lambda h, tl=tl: h.memset(
                v_sb[:, 1040 * tl:1040 * (tl + 1)].rearrange(
                    "p (hh ee) -> p hh ee", hh=16)[:, :, 64:65], 1.0),
                 writes=[f"v{tl}"], inc=1)
            for hh in range(H):
                sl, skey, tp = s_region(hh)
                P.op("tensor", lambda h, hh=hh, tl=tl, sl=sl, tp=tp: h.matmul(
                    sl,
                    k_sb[:, 1024 * tl + 64 * hh:1024 * tl + 64 * (hh + 1)],
                    v_sb[:, 1040 * tl + 65 * hh:1040 * tl + 65 * (hh + 1)],
                    start=True, stop=True, tile_position=tp, skip_group_check=True),
                     reads=[f"k{tl}", f"v{tl}"],
                     writes=[skey] if hh in (0, 1, 12, 13, 14, 15) else [],
                     inc=1 if hh in (0, 1, 12, 13, 14, 15) else 0)
            if t == 0:
                P.op("vector", lambda h: h.tensor_copy(pf_sb[:, 0:455], pS[:, 0:455]),
                     reads=["pS0"], writes=["pf0"], inc=1)
                P.op("vector", lambda h: h.tensor_copy(pf_sb[:, 455:520], pS[:, 512:577]),
                     reads=["pS1"], writes=["pf0"], inc=1)
            else:
                prev = carry if tl == 0 else pf_sb[:, 520 * (tl - 1):520 * tl]
                pkey = "carry" if tl == 0 else f"pf{tl - 1}"
                P.op("vector", lambda h, tl=tl, prev=prev: h.tensor_add(
                    pf_sb[:, 520 * tl:520 * tl + 455], prev[:, 0:455], pS[:, 0:455]),
                     reads=["pS0", pkey], writes=[f"pf{tl}"], inc=1)
                P.op("vector", lambda h, tl=tl, prev=prev: h.tensor_add(
                    pf_sb[:, 520 * tl + 455:520 * tl + 520], prev[:, 455:520], pS[:, 512:577]),
                     reads=["pS1", pkey], writes=[f"pf{tl}"], inc=1)
            if tl == 7:
                P.op("vector", lambda h: h.tensor_copy(carry[:, :], pf_sb[:, 520 * 7:520 * 8]),
                     reads=["pf7"], writes=["carry"], inc=1)
                dma("gpsimd", d_xnb[:, :, 1024 * sc:1024 * (sc + 1)],
                    xnb[:, :].rearrange("p (ec c) -> p ec c", ec=8),
                    [f"xnb{i}" for i in range(8)], [])
                dma("gpsimd", d_k[:, 8 * sc:8 * (sc + 1), :],
                    k_sb[:, :].rearrange("p (tt c) -> p tt c", tt=8),
                    [f"k{i}" for i in range(8)], [])
                dma("gpsimd", d_v[:, 8 * sc:8 * (sc + 1), :],
                    v_sb[:, :].rearrange("p (tt c) -> p tt c", tt=8),
                    [f"v{i}" for i in range(8)], [])
                dma("gpsimd", d_pf[:, 8 * sc:8 * (sc + 1), :],
                    pf_sb[:, :].rearrange("p (cc c) -> p cc c", cc=8),
                    [f"pf{i}" for i in range(8)], [])

        dma("gpsimd", cc_in.rearrange("(g p) e -> p g e", p=128),
            pf_sb[:, 520 * 7:520 * 8].rearrange("p (g e) -> p g e", g=8),
            ["pf7"], ["cc_in"])
        P.op("gpsimd", lambda h: h.collective_compute(
            "AllGather", ALU.bypass,
            replica_groups=[[0, 1], [2, 3], [4, 5], [6, 7]],
            ins=[cc_in], outs=[cc_out]),
             reads=["cc_in"], writes=["cc_out"], sem_name=CC, inc=1)
        dma("sync", sinit[:, :].rearrange("p (g e) -> p g e", g=8),
            cc_out[0:1024, :].rearrange("(g p) e -> p g e", p=128),
            ["cc_out"], ["sinit_r"])
        P.op("vector", lambda h: h.tensor_scalar_mul(sinit[:, :], sinit[:, :], st[:, 2:3]),
             reads=["sinit_r", "st_mk"], writes=["sinit"], inc=1)
        P.op("vector", lambda h: h.tensor_copy(sinit_bf[:, :], sinit[:, :]),
             reads=["sinit"], writes=["sinitb"], inc=1)

        q_sb = wq_ar

        for sc in range(NSC):
            xnb_keys = [f"xnb{i}" for i in range(8)]
            dma("sync", xnb[:, :].rearrange("p (ec c) -> p ec c", ec=8),
                d_xnb[:, :, 1024 * sc:1024 * (sc + 1)], [], xnb_keys)
            dma("sync", k_sb[:, :].rearrange("p (tt c) -> p tt c", tt=8),
                d_k[:, 8 * sc:8 * (sc + 1), :], [], [f"k{i}" for i in range(8)])
            dma("sync", v_sb[:, :].rearrange("p (tt c) -> p tt c", tt=8),
                d_v[:, 8 * sc:8 * (sc + 1), :], [], [f"v{i}" for i in range(8)])
            lo = max(0, 8 * sc - 1)
            pos0 = 1 if sc == 0 else 0
            n_pf = 8 - pos0 if sc > 0 else 7
            n_pf = 8 * sc + 7 - lo
            dma("sync", pf_sb[:, 520 * pos0:520 * (pos0 + n_pf)].rearrange(
                "p (cc c) -> p cc c", cc=n_pf),
                d_pf[:, lo:lo + n_pf, :], [], [f"pf{i}" for i in range(8)])
            P.op("vector", lambda h, pos0=pos0, n_pf=n_pf: h.tensor_tensor(
                pf_sb[:, 520 * pos0:520 * (pos0 + n_pf)].rearrange("p (cc c) -> p cc c", cc=n_pf),
                pf_sb[:, 520 * pos0:520 * (pos0 + n_pf)].rearrange("p (cc c) -> p cc c", cc=n_pf),
                sinit[:, :].unsqueeze(1).broadcast_to([128, n_pf, 520]), ALU.add),
                 reads=["sinit"] + [f"pf{i}" for i in range(8)],
                 writes=[f"pf{i}" for i in range(8)], inc=1)
            P.op("vector", lambda h, pos0=pos0, n_pf=n_pf: h.tensor_copy(
                pf_bf[:, 520 * pos0:520 * (pos0 + n_pf)],
                pf_sb[:, 520 * pos0:520 * (pos0 + n_pf)]),
                 reads=[f"pf{i}" for i in range(8)], writes=["pfb"], inc=1)

            for qc in range(8):
                lob, lkey = (pA, "pA") if qc % 2 == 0 else (pC, "pC")
                hib, hkey = (pB, "pB") if qc % 2 == 0 else (pD, "pD")
                for e in range(8):
                    P.op("tensor", lambda h, e=e, qc=qc, lob=lob: h.matmul(
                        lob[:, :], watq[:, 1024 * e + 128 * qc:1024 * e + 128 * (qc + 1)],
                        xnb[:, 1024 * e:1024 * e + 512],
                        start=(e == 0), stop=(e == 7), skip_group_check=True),
                         reads=["w_q"] + xnb_keys,
                         writes=[lkey] if e in (0, 7) else [], inc=1 if e in (0, 7) else 0)
                    P.op("tensor", lambda h, e=e, qc=qc, hib=hib: h.matmul(
                        hib[:, :], watq[:, 1024 * e + 128 * qc:1024 * e + 128 * (qc + 1)],
                        xnb[:, 1024 * e + 512:1024 * e + 1024],
                        start=(e == 0), stop=(e == 7), skip_group_check=True),
                         reads=["w_q"] + xnb_keys,
                         writes=[hkey] if e in (0, 7) else [], inc=1 if e in (0, 7) else 0)
                for half, (bank, bkey) in enumerate(((lob, lkey), (hib, hkey))):
                    pe = psc[:, 1024 * (qc % 2):1024 * (qc % 2) + 512]
                    pr = psc[:, 1024 * (qc % 2) + 512:1024 * (qc % 2) + 1024]
                    pky = f"psc{qc % 2}"
                    P.op("scalar", lambda h, bank=bank, pe=pe: h.activation(pe, bank[:, :], AF.Exp),
                         reads=[bkey], writes=[pky + "e"], inc=1)
                    P.op("scalar", lambda h, bank=bank, pr=pr: h.activation(pr, bank[:, :], AF.Relu),
                         reads=[bkey], writes=[pky + "r"], inc=1)
                    P.op("vector", lambda h, pe=pe, pr=pr, qc=qc, half=half: h.scalar_tensor_tensor(
                        q_sb[:, 1024 * qc + 512 * half:1024 * qc + 512 * (half + 1)],
                        pe, 1.0, pr, ALU.min, ALU.add),
                         reads=[pky + "e", pky + "r", bkey], writes=[f"q{qc}"], inc=1)

            for e in range(8):
                P.op("scalar", lambda h, e=e: h.activation(
                    xq[:, 1024 * e:1024 * (e + 1)], xnb[:, 1024 * e:1024 * (e + 1)],
                    AF.Copy, scale=cols[:, 1:2]),
                     reads=xnb_keys + ["c_si"], writes=[f"xq{e}"], inc=1)
            for tt in range(8):
                lob, lkey = (pA, "pA") if tt % 2 == 0 else (pC, "pC")
                hib, hkey = (pB, "pB") if tt % 2 == 0 else (pD, "pD")
                for e in range(8):
                    P.op("tensor", lambda h, e=e, tt=tt, lob=lob: h.matmul(
                        lob[:, :], xq[:, 1024 * e + 128 * tt:1024 * e + 128 * (tt + 1)],
                        kq[:, 1024 * e:1024 * e + 512],
                        start=(e == 0), stop=(e == 7), skip_group_check=True),
                         reads=[f"xq{e}", "kq"],
                         writes=[lkey] if e in (0, 7) else [], inc=1 if e in (0, 7) else 0)
                    P.op("tensor", lambda h, e=e, tt=tt, hib=hib: h.matmul(
                        hib[:, :], xq[:, 1024 * e + 128 * tt:1024 * e + 128 * (tt + 1)],
                        kq[:, 1024 * e + 512:1024 * e + 1024],
                        start=(e == 0), stop=(e == 7), skip_group_check=True),
                         reads=[f"xq{e}", "kq"],
                         writes=[hkey] if e in (0, 7) else [], inc=1 if e in (0, 7) else 0)
                for half, (bank, bkey) in enumerate(((lob, lkey), (hib, hkey))):
                    j = 2 * (tt % 2) + half
                    dst = scm[:, 512 * j:512 * (j + 1)]
                    sky = "scmA" if j < 2 else "scmB"
                    P.op("scalar", lambda h, bank=bank, dst=dst: h.activation(
                        dst, bank[:, :], AF.Copy, scale=cols[:, 3:4]),
                         reads=[bkey, "c_sp"], writes=[sky], inc=1)
                    P.op("vector", lambda h, dst=dst, tt=tt, half=half: h.tensor_add(
                        q_sb[:, 8192 + 1024 * tt + 512 * half:8192 + 1024 * tt + 512 * (half + 1)],
                        dst, fbb[:, 512 * half:512 * (half + 1)]),
                         reads=[sky, "fbb"], writes=[f"ffn{tt}"], inc=1)

            for c in range(8):
                for g in range(8):
                    P.op("tensor", lambda h, g=g, c=c: h.transpose(
                        tr1[:, 128 * g:128 * (g + 1)],
                        k_sb[:, 1024 * c + 128 * g:1024 * c + 128 * (g + 1)], ident[:, :]),
                         reads=[f"k{c}", "ident"], writes=["tr1"] if g in (0, 7) else [],
                         inc=1 if g in (0, 7) else 0)
                P.op("vector", lambda h: h.tensor_copy(kf[:, :], tr1[:, :]),
                     reads=["tr1"], writes=["kf"], inc=1)
                for rnd in range(2):
                    for hh in range(8 * rnd, 8 * rnd + 8):
                        g, par = hh // 2, hh % 2
                        bank, bkey = (pA, "pA") if par == 0 else (pB, "pB")
                        slt = (hh // 2) % 4
                        P.op("tensor", lambda h, g=g, par=par, c=c, bank=bank, slt=slt:
                             h.matmul(
                                 bank[:, 128 * slt:128 * (slt + 1)],
                                 kf[64 * par:64 * (par + 1), 128 * g:128 * (g + 1)],
                                 q_sb[64 * par:64 * (par + 1),
                                      1024 * g + 128 * c:1024 * g + 128 * (c + 1)],
                                 start=True, stop=True, skip_group_check=True),
                             reads=["kf", f"q{g}"],
                             writes=[bkey] if slt in (0, 3) else [],
                             inc=1 if slt in (0, 3) else 0)
                    sky = "scmA" if rnd == 0 else "scmB"
                    scm4 = scm[:, :].rearrange("p (r2 s par c2) -> p r2 s par c2",
                                               r2=2, s=4, par=2)
                    for par, (bank, bkey) in enumerate(((pA, "pA"), (pB, "pB"))):
                        P.op("vector", lambda h, bank=bank, rnd=rnd, par=par, scm4=scm4:
                             h.tensor_tensor(
                                 scm4[:, rnd, :, par, :],
                                 bank[:, :].rearrange("p (s c2) -> p s c2", s=4),
                                 mask[:, :].unsqueeze(1).broadcast_to([128, 4, 128]),
                                 ALU.mult),
                             reads=[bkey, "mask"], writes=[sky], inc=1)
                    for hh in range(8 * rnd, 8 * rnd + 8):
                        g, par = hh // 2, hh % 2
                        if hh < 7:
                            nb, nkey, noff = pC, "pC", 65 * hh
                        elif hh < 14:
                            nb, nkey, noff = pD, "pD", 65 * (hh - 7)
                        else:
                            nb, nkey, noff = pS, "pS0", 65 * (hh - 14)
                        P.op("tensor", lambda h, hh=hh, c=c, nb=nb, noff=noff: h.matmul(
                            nb[:, noff:noff + 65],
                            scm[:, 128 * hh:128 * (hh + 1)],
                            v_sb[:, 1040 * c + 65 * hh:1040 * c + 65 * (hh + 1)],
                            start=True, stop=False, skip_group_check=True),
                             reads=[sky, f"v{c}"], writes=[nkey], inc=1)
                        if sc == 0 and c == 0:
                            rhs = sinit_bf[64 * par:64 * (par + 1), pf_col(g):pf_col(g) + 65]
                            rkey = "sinitb"
                        else:
                            pfo = 520 * c + pf_col(g)
                            rhs = pf_bf[64 * par:64 * (par + 1), pfo:pfo + 65]
                            rkey = "pfb"
                        P.op("tensor", lambda h, g=g, par=par, c=c, nb=nb, noff=noff, rhs=rhs:
                             h.matmul(
                                 nb[:, noff:noff + 65],
                                 q_sb[64 * par:64 * (par + 1),
                                      1024 * g + 128 * c:1024 * g + 128 * (c + 1)],
                                 rhs, start=False, stop=True, skip_group_check=True),
                             reads=[f"q{g}", rkey], writes=[nkey], inc=1)
                P.op("vector", lambda h: h.tensor_copy(st[:, 32:39], pC[:, 64:455:65]),
                     reads=["pC"], writes=["dens"], inc=1)
                P.op("vector", lambda h: h.tensor_copy(st[:, 39:46], pD[:, 64:455:65]),
                     reads=["pD"], writes=["dens"], inc=1)
                P.op("vector", lambda h: h.tensor_copy(st[:, 46:48], pS[:, 64:194:65]),
                     reads=["pS0"], writes=["dens"], inc=1)
                P.op("vector", lambda h: h.tensor_scalar_add(st[:, 32:48], st[:, 32:48], 1e-6),
                     reads=["dens"], writes=["dens2"], inc=1)
                P.op("vector", lambda h: h.reciprocal(st[:, 48:64], st[:, 32:48]),
                     reads=["dens2"], writes=["rec"], inc=1)
                for nb, nkey, h0, nh, ybase in ((pC, "pC", 0, 7, 0), (pD, "pD", 7, 7, 448),
                                                (pS, "pS0", 14, 2, 896)):
                    P.op("vector", lambda h, nb=nb, h0=h0, nh=nh, ybase=ybase: h.tensor_tensor(
                        yt[:, ybase:ybase + 64 * nh].rearrange("p (s e) -> p s e", s=nh),
                        nb[:, 0:65 * nh].rearrange("p (s e) -> p s e", s=nh)[:, :, 0:64],
                        st[:, 48 + h0:48 + h0 + nh].unsqueeze(2).broadcast_to([128, nh, 64]),
                        ALU.mult),
                         reads=[nkey, "rec"], writes=["yt"], inc=1)
                for g in range(8):
                    P.op("tensor", lambda h, g=g: h.transpose(
                        tr2[:, 128 * g:128 * (g + 1)], yt[:, 128 * g:128 * (g + 1)], ident[:, :]),
                         reads=["yt", "ident"], writes=["tr2"] if g in (0, 7) else [],
                         inc=1 if g in (0, 7) else 0)
                P.op("vector", lambda h, c=c: h.tensor_copy(
                    xnb[:, :].rearrange("p (ec c2) -> p ec c2", ec=8)[:, :, 128 * c:128 * (c + 1)],
                    tr2[:, :].rearrange("p (ec c2) -> p ec c2", ec=8)),
                     reads=["tr2"], writes=[f"xnb{c}"], inc=1)

            for tt in range(8):
                tglob = 8 * sc + tt
                slot = tt % 2
                xts = xt[:, 1024 * slot:1024 * (slot + 1)]
                xkey = f"xt{slot}"
                dma("sync", xts, x[128 * tglob:128 * (tglob + 1), :], [], [xkey])
                lob, lkey = (pA, "pA") if tt % 2 == 0 else (pC, "pC")
                hib, hkey = (pB, "pB") if tt % 2 == 0 else (pD, "pD")
                for e in range(8):
                    P.op("tensor", lambda h, e=e, tt=tt, lob=lob: h.matmul(
                        lob[:, :], xnb[:, 1024 * e + 128 * tt:1024 * e + 128 * (tt + 1)],
                        wprT[:, 1024 * e:1024 * e + 512],
                        start=(e == 0), stop=(e == 7), skip_group_check=True),
                         reads=[f"xnb{tt}", "w_pr"],
                         writes=[lkey] if e in (0, 7) else [], inc=1 if e in (0, 7) else 0)
                    P.op("tensor", lambda h, e=e, tt=tt, hib=hib: h.matmul(
                        hib[:, :], xnb[:, 1024 * e + 128 * tt:1024 * e + 128 * (tt + 1)],
                        wprT[:, 1024 * e + 512:1024 * e + 1024],
                        start=(e == 0), stop=(e == 7), skip_group_check=True),
                         reads=[f"xnb{tt}", "w_pr"],
                         writes=[hkey] if e in (0, 7) else [], inc=1 if e in (0, 7) else 0)
                ot = outt[:, 1024 * slot:1024 * (slot + 1)]
                ot2 = outt2[:, 1024 * slot:1024 * (slot + 1)]
                okey, okey2 = f"ot{slot}", f"ot2_{slot}"
                P.op("vector", lambda h, ot=ot, xts=xts, lob=lob: h.tensor_add(
                    ot[:, 0:512], xts[:, 0:512], lob[:, :]),
                     reads=[xkey, lkey], writes=[okey], inc=1)
                P.op("vector", lambda h, ot=ot, xts=xts, hib=hib: h.tensor_add(
                    ot[:, 512:1024], xts[:, 512:1024], hib[:, :]),
                     reads=[xkey, hkey], writes=[okey], inc=1)
                P.op("gpsimd", lambda h, ot=ot, ot2=ot2, tt=tt: h.tensor_tensor(
                    ot2[:, :], ot[:, :],
                    q_sb[:, 8192 + 1024 * tt:8192 + 1024 * (tt + 1)], ALU.add),
                     reads=[okey, f"ffn{tt}"], writes=[okey2], inc=1)
                dma("gpsimd", out[128 * tglob:128 * (tglob + 1), :], ot2[:, :], [okey2], [])

        P.raw("gpsimd", lambda h: h.wait_ge(P.sems[STQ], P.cnt[STQ]))
        P.raw("sync", lambda h: h.wait_ge(P.sems[LD], P.cnt[LD]))
        P.flush()
    return nc


def kernel(L0_attn_x, L0_attn_w_attn, L0_attn_w_proj, L0_ffn_kernel,
           L0_ffn_input_scale, L0_ffn_kernel_scale, L0_ffn_bias):
    x = np.ascontiguousarray(np.asarray(L0_attn_x, dtype=np.float32))
    wat = np.ascontiguousarray(np.asarray(L0_attn_w_attn, dtype=np.float32))
    wpr = np.ascontiguousarray(np.asarray(L0_attn_w_proj, dtype=np.float32))
    fk = np.ascontiguousarray(np.asarray(L0_ffn_kernel))
    fb = np.asarray(L0_ffn_bias)
    s_in = np.asarray(L0_ffn_input_scale, dtype=np.float32).ravel()[0]
    s_k = np.asarray(L0_ffn_kernel_scale, dtype=np.float32).ravel()[0]
    sin_col = np.ascontiguousarray(np.broadcast_to(s_in, (128, 1))).astype(np.float32)
    sk_col = np.ascontiguousarray(np.broadcast_to(s_k, (128, 1))).astype(np.float32)
    fbr = np.ascontiguousarray(np.broadcast_to(fb.reshape(1, E), (128, E)))

    nc = build_nc()
    in_maps = []
    for c in range(NCORES):
        b, half = divmod(c, 2)
        xs = np.ascontiguousarray(x[b, half * T:(half + 1) * T, :])
        mk = np.full((128, 1), 1.0 if half == 1 else 0.0, dtype=np.float32)
        in_maps.append({"x": xs, "wat": wat, "wpr": wpr, "fk": fk,
                        "s_in": sin_col, "s_k": sk_col, "fbr": fbr, "mkv": mk})
    res = run_bass_kernel_spmd(nc, in_maps, core_ids=list(range(NCORES)))
    outp = np.empty((4, 8192, E), dtype=np.float32)
    for c in range(NCORES):
        b, half = divmod(c, 2)
        outp[b, half * T:(half + 1) * T, :] = res.results[c]["out"]
    return outp
